# version 10
# baseline (speedup 1.0000x reference)
import numpy as np
from contextlib import ExitStack

import ml_dtypes
import concourse.bass as bass
import concourse.mybir as mybir
from concourse.mybir import AluOpType as A
from concourse.bass_utils import run_bass_kernel_spmd
from concourse.bass_isa import ReduceOp

F32 = mybir.dt.float32
BF16 = mybir.dt.bfloat16
FP16 = mybir.dt.float16
FP8 = mybir.dt.float8e4
AF = mybir.ActivationFunctionType
AX = mybir.AxisListType

B, L, D = 2, 8192, 1024
NH, FD, HD, INNER = 16, 16, 64, 4096
NCORE = 8
C = 128
OWN_CH = 16
PRE_CH = 48
NGRP = 4
R23 = float(2 ** 23)

DEBUG_TAPS = []
STAGE = 4
SLOTBAR = True
PREP_UPFRONT = True
DBG_ON = set()
ASUB = 3
OVERRIDE_CH = None


def _sv(t, offset, pattern):
    h = t.tensor if isinstance(t, bass.AP) else t
    return bass.AP(h, offset, pattern)


class Ctr:
    def __init__(self, nc, es, name, step=1):
        self.sem = es.enter_context(nc.semaphore(name))
        self.n = 0
        self.step = step

    def tick(self, inst):
        inst.then_inc(self.sem, self.step)
        self.n += self.step
        return (self, self.n)


class Stream:
    def __init__(self, name, ctr):
        self.name = name
        self.eng = None
        self.ctr = ctr
        self.waited = {}

    def wait(self, *toks):
        for tok in toks:
            if callable(tok):
                tok = tok()
            if tok is None:
                continue
            ctr, val = tok
            if ctr is self.ctr and ctr.step == 1:
                continue
            if self.waited.get(id(ctr), -1) >= val:
                continue
            self.eng.wait_ge(ctr.sem, val)
            self.waited[id(ctr)] = val

    def done(self, inst):
        return self.ctr.tick(inst)

    def gate(self, *toks):
        for tok in toks:
            if callable(tok):
                tok = tok()
            if tok is None:
                continue
            ctr, val = tok
            if self.waited.get(id(ctr), -1) >= val:
                continue
            self.eng.wait_ge(ctr.sem, val)
            self.waited[id(ctr)] = val


def build(pre_ch, own_ch, taps=()):
    nc = bass.Bass("TRN2", target_bir_lowering=False, debug=False,
                   detect_race_conditions=False)
    tot_ch = pre_ch + own_ch

    def din(name, shape, dt=F32):
        return nc.dram_tensor(name, shape, dt, kind="ExternalInput").ap()

    x_pre = din("x_pre", [max(pre_ch, 1) * C, D])
    x_own = din("x_own", [own_ch * C, D])
    wq = din("wq", [D, NH * FD])
    wk = din("wk", [D, NH * FD])
    wv = din("wv", [D, D])
    wo = din("wo", [D, D])
    wgT = din("wgT", [D, INNER])
    wuT = din("wuT", [D, INNER])
    wdT = din("wdT", [INNER, D])
    ident = din("ident", [C, C], BF16)
    mask = din("maskM", [C, C], BF16)
    ones_pad = din("ones_pad", [C, 16], BF16)
    zeros_pad = din("zeros_pad", [C, 32], BF16)
    vflag = din("vflag", [tot_ch, C, NH], BF16)
    zstate = din("zstate", [C, 1040])
    onesf = din("onesf", [C, 1])
    out = nc.dram_tensor("out", [own_ch * C, D], F32, kind="ExternalOutput").ap()
    iw_dh = nc.dram_tensor("iw_dh", [INNER, D], FP8, kind="Internal").ap()
    wmean_h = nc.dram_tensor("wmean_h", [4], F32, kind="Internal").ap()
    tapaps = {}
    for tname, tshape, tdt in [t for t in taps if isinstance(t, tuple)]:
        tapaps[tname] = nc.dram_tensor("tap_" + tname, tshape, tdt, kind="ExternalOutput").ap()

    es = ExitStack()

    def sb(name, shape, dt=F32):
        return es.enter_context(nc.sbuf_tensor(name, shape, dt))

    def ps(name, shape, dt=F32):
        return es.enter_context(nc.psum_tensor(name, shape, dt))

    wq_sb = sb("wq_sb", [C, 8 * 256], BF16)
    wk_sb = sb("wk_sb", [C, 8 * 256], BF16)
    wv_sb = sb("wv_sb", [C, 8 * 1024], BF16)
    wo_sb = sb("wo_sb", [C, 8 * 1024], BF16)
    iw_g = sb("iw_g", [C, 8 * INNER], FP8)
    iw_u = sb("iw_u", [C, 8 * INNER], FP8)
    iw_dstg = sb("iw_dstg", [C, 1024], FP8)
    iw_dbuf = [sb(f"iw_dbuf{i}", [C, 8 * D], FP8) for i in range(2)]
    xbuf = [sb(f"xbuf{i}", [C, D]) for i in range(2)]
    x2 = [sb("x2_0", [C, D])] * 2
    x3 = sb("x3", [C, D])
    ln1 = sb("ln1", [C, D], BF16)
    ln1T = sb("ln1T", [C, D], BF16)
    qtok = sb("qtok", [C, 512], BF16)
    qtokH = sb("qtokH", [C, 512], BF16)
    ktok4 = sb("ktok4", [C, 512], BF16)
    qTb = sb("qTb", [C, 1024], BF16)
    kTb = sb("kTb", [C, 1024], BF16)
    kbar = sb("kbar", [C, 272], BF16)
    vbar = sb("vbar", [C, NH * 65], BF16)
    kf2 = sb("kf2", [C, 1024], BF16)
    qf2 = sb("qf2", [C, 1024], BF16)
    qf2T = sb("qf2T", [C, 1024], BF16)
    upoly = sb("upoly", [C, 512], BF16)
    abar = sb("abar", [C, 512], BF16)
    ytok = sb("ytok", [C, D], BF16)
    yT = sb("yT", [C, D], BF16)
    ln2 = sb("ln2", [C, D], FP16)
    iq1T = sb("iq1T", [C, D], BF16)
    silu_s = sb("silu_s", [C, 512], FP16)
    gpp = sb("gpp", [C, INNER], BF16)
    iqgT = [sb(f"iqgT{i}", [C, 1024], BF16) for i in range(2)]
    
    s_all = sb("s_all", [C, 3120])
    s_allb = sb("s_allb", [C, 3120], BF16)
    wstage = sb("wstage", [C, 1024])
    wpart = sb("wpart", [C, 96])
    wsum = sb("wsum", [C, 4])
    swg = sb("swg", [C, 4])
    bnb = sb("bnb", [C, 12])
    onesf_sb = sb("onesf_sb", [C, 1])
    vstg = sb("vstg", [C, NH], BF16)
    padstg = sb("padstg", [C, 48], BF16)
    tbuf = sb("tbuf", [C, 4])
    agg = sb("agg", [C, 2])
    sc = sb("sc", [C, 16])
    dnm = sb("dnm", [C, 8])
    zrec = sb("zrec", [C, 8])
    idn = sb("idn", [C, C], BF16)
    msk = sb("msk", [C, C], BF16)

    p_proj = ps("p_proj", [C, 512])
    p_da = ps("p_da", [C, 512])
    p_db = ps("p_db", [C, 512])
    p_tr = ps("p_tr", [C, 1024], BF16)
    p_y = ps("p_y", [C, 512])
    p_g = ps("p_g", [C, 512])
    p_u = ps("p_u", [C, 512])
    p_sr = ps("p_sr", [C, 512])

    CT = Ctr(nc, es, "ct")
    CV = Ctr(nc, es, "cv")
    CS = Ctr(nc, es, "cs")
    CG = Ctr(nc, es, "cg")
    DS = Ctr(nc, es, "ds", 16)
    DG = Ctr(nc, es, "dg", 16)
    ctrs = {"sync": DS, "gpsimd": CG, "vector": CV, "scalar": CS, "tensor": CT}

    program = []

    def emit(en, fn):
        program.append((en, fn))

    G = {
        "xload": {}, "xfree": {}, "vread_prev": None, "state_add": None,
        "y_read": None, "vones": None, "ptr_free": None, "poly_last": None,
        "yevac_last": None, "pg_free": None, "out_last": None, "const": None,
        "attnw": None, "sw": {}, "quant_done": None, "iq1T_free": None,
    }

    def prologue(s):
        s.done(s.eng.dma_start(out=idn[:], in_=ident[:]))
        s.done(s.eng.dma_start(out=msk[:], in_=mask[:]))
        s.done(s.eng.dma_start(out=s_all[:, 0:1040], in_=zstate[:]))
        s.done(s.eng.dma_start(out=s_all[:, 1040:2080], in_=zstate[:]))
        s.done(s.eng.dma_start(out=s_all[:, 2080:3120], in_=zstate[:]))
        s.done(s.eng.dma_start(out=onesf_sb[:], in_=onesf[:]))
        s.done(s.eng.dma_start(out=padstg[:, 0:16], in_=ones_pad[:]))
        G["const0"] = s.done(s.eng.dma_start(out=padstg[:, 16:48], in_=zeros_pad[:]))
    emit("sync", prologue)

    def prologue_g(s):
        for (w_dram, w_sb, fc) in ((wq, wq_sb, 256), (wk, wk_sb, 256),
                                   (wv, wv_sb, 1024), (wo, wo_sb, 1024)):
            last = s.eng.dma_start(out=w_sb[:],
                                   in_=_sv(w_dram, 0, [[fc, C], [C * fc, 8], [1, fc]]))
            G["attnw"] = DG.tick(last)
    emit("gpsimd", prologue_g)

    def prologue_v(s):
        s.wait(G["const0"])
        inst = None
        for buf in (qtok, qtokH):
            inst = s.eng.tensor_copy(out=_sv(buf, 16, [[512, C], [32, 16], [1, 1]]),
                                     in_=_sv(padstg, 0, [[48, C], [1, 16], [0, 1]]))
            inst = s.eng.tensor_copy(out=_sv(buf, 17, [[512, C], [32, 16], [1, 15]]),
                                     in_=_sv(padstg, 16, [[48, C], [0, 16], [1, 15]]))
        inst = s.eng.tensor_copy(out=_sv(ktok4, 16, [[512, C], [32, 16], [1, 16]]),
                                 in_=_sv(padstg, 16, [[48, C], [0, 16], [1, 16]]))
        inst = s.eng.tensor_copy(out=_sv(kbar, 16, [[272, C], [17, 16], [1, 1]]),
                                 in_=_sv(padstg, 0, [[48, C], [1, 16], [0, 1]]))
        G["const"] = s.done(inst)
    emit("vector", prologue_v)

    def emit_xload(slot):
        def fn(s):
            s.wait(G["xfree"].get(slot - 2))
            if slot < pre_ch:
                src = x_pre[slot * C:(slot + 1) * C, :]
            else:
                j = slot - pre_ch
                src = x_own[j * C:(j + 1) * C, :]
            G["xload"][slot] = s.done(s.eng.dma_start(out=xbuf[slot % 2][:], in_=src))
        emit("sync", fn)

    P1 = [("g", wgT, 8, 4096, 0), ("u", wuT, 8, 4096, 1), ("d", wdT, 32, 1024, 2)]
    p1_units = []
    p2_units = []
    _pc = [0]
    for mn, src, ntile, ncol, mi in P1:
        base_col = _pc[0]
        nh = (ncol + 1023) // 1024
        for t in range(ntile):
            for hh in range(nh):
                co = hh * 1024
                w = min(1024, ncol - co)
                p1_units.append(("tile", (mn, src, t, co, w, _pc[0])))
                p2_units.append((mn, src, t, co, w, mi, ncol))
                _pc[0] += 1
        p1_units.append(("fin", (mn, src, mi, base_col, _pc[0] - 1)))
    assert _pc[0] <= 96, _pc[0]
    wtok = {"dma": None, "used": None, "red": None}

    def emit_p1_unit(u):
        kind, pl = u
        if kind == "tile":
            mn, src, t, co, w, col = pl

            def fdma(s, src=src, t=t, co=co, w=w):
                s.wait(wtok["red"], wtok["used"])
                wtok["dma"] = s.done(s.eng.dma_start(out=wstage[:, 0:w],
                                                     in_=src[t * C:(t + 1) * C, co:co + w]))
            emit("sync", fdma)

            def fabs(s, col=col, w=w):
                s.wait(wtok["dma"], G.get("dn_last"))
                inst = s.eng.activation(out=yT[:, 0:w], in_=wstage[:, 0:w],
                                        func=AF.Abs, bias=0.0, scale=1.0,
                                        accum_out=wpart[:, col:col + 1])
                wtok["red"] = s.done(inst)
            emit("scalar", fabs)
        else:
            mn, src, mi, c0, c1 = pl

            def ffin(s, mn=mn, src=src, mi=mi, c0=c0, c1=c1):
                s.wait(wtok["red"])
                inst = s.eng.tensor_reduce(out=wsum[:, mi:mi + 1],
                                           in_=wpart[:, c0:c1 + 1], axis=AX.X, op=A.add)
                G["sw"][mn + "_p"] = s.done(inst)
            emit("vector", ffin)

            def fmm(s, mn=mn, mi=mi):
                s.wait(G["sw"][mn + "_p"], G["const"],
                       G["sw"].get("g_cp"), G["sw"].get("u_cp"))
                s.eng.drain()
                inst = s.eng.matmul(out=p_sr[0:1, 0:1], lhsT=wsum[:, mi:mi + 1],
                                    rhs=onesf_sb[:], start=True, stop=True)
                G["sw"][mn + "_mm"] = s.done(inst)
            emit("tensor", fmm)

            def fcp(s, mn=mn, mi=mi):
                s.wait(G["sw"][mn + "_mm"])
                inst = s.eng.activation(out=tbuf[0:1, mi:mi + 1], in_=p_sr[0:1, 0:1],
                                        func=AF.Copy, bias=0.0, scale=1.0)
                G["sw"][mn + "_cp"] = s.done(inst)
            emit("scalar", fcp)

            def fbounce(s, mn=mn, mi=mi):
                s.wait(G["sw"][mn + "_cp"])
                tok = s.done(s.eng.dma_start(out=wmean_h[mi:mi + 1], in_=tbuf[0:1, mi:mi + 1]))
                s.wait(tok)
                inst = s.eng.dma_start(out=sc[:, 12 + mi:13 + mi],
                                       in_=_sv(wmean_h, mi, [[0, C], [1, 1]]),
                                       allow_slow_non_contiguous=True)
                G["sw"][mn + "_bc"] = s.done(inst)
            emit("sync", fbounce)

            def fsc(s, mn=mn, src=src, mi=mi):
                s.wait(G["sw"][mn + "_bc"])
                nelem = float(src.shape[0] * src.shape[1])
                inst = s.eng.tensor_scalar(out=sc[:, 12 + mi:13 + mi], in0=sc[:, 12 + mi:13 + mi],
                                           scalar1=1.0 / nelem, scalar2=1e-5, op0=A.mult, op1=A.max)
                tok = s.done(inst)
                s.eng.wait_ge(CV.sem, tok[1])
                inst = s.eng.reciprocal(out=swg[:, mi:mi + 1], in_=sc[:, 12 + mi:13 + mi])
                G["sw"][mn] = s.done(inst)
            emit("vector", fsc)

    def emit_p2_unit(u):
        mn, src, t, co, w, mi, ncol = u

        def fdma(s, src=src, t=t, co=co, w=w):
            s.wait(wtok["used"], wtok["red"])
            wtok["dma"] = s.done(s.eng.dma_start(out=wstage[:, 0:w],
                                                 in_=src[t * C:(t + 1) * C, co:co + w]))
        emit("sync", fdma)

        def fq(s, mn=mn, t=t, co=co, w=w, mi=mi, ncol=ncol):
            s.wait(wtok["dma"], G.get("dn_last"))
            s.gate(G["sw"].get(mn))
            if mn == "d":
                s.wait(wtok.get("dstg_free"))
            s.eng.tensor_scalar(out=gpp[:, 0:w], in0=wstage[:, 0:w],
                                scalar1=swg[:, mi:mi + 1], scalar2=0.5, op0=A.mult, op1=A.is_gt)
            inst = s.eng.tensor_scalar(out=gpp[:, 2048:2048 + w], in0=wstage[:, 0:w],
                                       scalar1=swg[:, mi:mi + 1], scalar2=-0.5, op0=A.mult, op1=A.is_lt)
            wtok["used"] = s.done(inst)
            if mn == "d":
                dst = iw_dstg[:, 0:w]
            else:
                dst = {"g": iw_g, "u": iw_u}[mn][:, t * ncol + co:t * ncol + co + w]
            inst = s.eng.tensor_tensor(out=dst,
                                       in0=gpp[:, 0:w], in1=gpp[:, 2048:2048 + w], op=A.subtract)
            G["quant_done"] = s.done(inst)
        emit("vector", fq)
        if mn == "d":
            def fstore(s, t=t):
                s.wait(G["quant_done"])
                tok = s.done(s.eng.dma_start(out=iw_dh[t * C:(t + 1) * C, :], in_=iw_dstg[:]))
                wtok["dstg_free"] = tok
                G["dq_store"] = tok
            emit("sync", fstore)

    def ln_chunk(xb, out_bf, out_f16, tok_x):
        toks = {}

        def f1(s):
            for t in G.get("bar") or ():
                s.wait(t)
            s.wait(tok_x, G.get("ln_rs_prev"))
            s.eng.bn_stats(out=bnb[:, 0:6], in_=xb[:, 0:512])
            tok = s.done(s.eng.bn_stats(out=bnb[:, 6:12], in_=xb[:, 512:1024]))
            s.eng.wait_ge(CV.sem, tok[1])
            tok = s.done(s.eng.bn_aggr(out=agg[:], in_=bnb[:]))
            s.eng.wait_ge(CV.sem, tok[1])
            tok = s.done(s.eng.tensor_scalar(out=sc[:, 0:1], in0=agg[:, 1:2], scalar1=1e-5,
                                             scalar2=None, op0=A.add))
            s.eng.wait_ge(CV.sem, tok[1])
            toks["rec"] = s.done(s.eng.reciprocal(out=sc[:, 1:2], in_=sc[:, 0:1]))
        emit("vector", f1)

        def f2(s):
            for t in G.get("bar") or ():
                s.wait(t)
            s.wait(toks["rec"])
            toks["rs"] = s.done(s.eng.activation(out=sc[:, 2:3], in_=sc[:, 1:2],
                                                 func=AF.Sqrt, bias=0.0, scale=1.0))

        def f2b(s):
            G["ln_rs_prev"] = toks["rs"]
        emit("scalar", f2)
        program.append(("vector", f2b))

        def f3(s):
            s.wait(toks["rs"])
            if out_bf is not None:
                s.wait(G.get("ln1_tr_prev"))
            last = None
            if out_bf is not None:
                last = s.eng.tensor_scalar(out=out_bf[:], in0=xb[:], scalar1=agg[:, 0:1],
                                           scalar2=sc[:, 2:3], op0=A.subtract, op1=A.mult)
            if out_f16 is not None:
                last = s.eng.tensor_scalar(out=out_f16[:], in0=xb[:], scalar1=agg[:, 0:1],
                                           scalar2=sc[:, 2:3], op0=A.subtract, op1=A.mult)
            toks["ln"] = s.done(last)
        emit("vector", f3)
        return toks

    def transpose_blk(src_bf, src_off_col, dst_bf, dst_ncols, tok_src, gkeys=(), bias=0.0):
        toks = {}

        def ft(s):
            s.wait(tok_src, G["ptr_free"], *(G.get(k) for k in gkeys))
            s.eng.drain()
            last = None
            for i in range(dst_ncols):
                cc = src_off_col + i * C
                last = s.eng.transpose(out=p_tr[:, i * C:(i + 1) * C],
                                       in_=src_bf[:, cc:cc + C], identity=idn[:])
            toks["tr"] = s.done(last)
        emit("tensor", ft)

        def fe(s):
            s.wait(toks["tr"])
            inst = s.eng.activation(out=dst_bf[:, 0:dst_ncols * C], in_=p_tr[:, 0:dst_ncols * C],
                                    func=AF.Copy, bias=bias, scale=1.0)
            toks["ev"] = s.done(inst)
            G["ptr_free"] = toks["ev"]
        emit("scalar", fe)
        return toks

    def tap(name, buf, wait=None):
        if name in tapaps:
            def fn(s):
                if wait is not None:
                    s.wait(wait)
                ap = buf if isinstance(buf, bass.AP) else buf[:]
                s.done(s.eng.dma_start(out=tapaps[name], in_=ap))
            emit("sync", fn)

    def emit_slot(slot):
        own = slot >= pre_ch
        xb = xbuf[slot % 2]
        x2b = x2[slot % 2]
        toks = {}

        def fvone(s):
            for t in G.get("bar") or ():
                s.wait(t)
            s.wait(G.get("vstg_done"))
            inst = s.eng.dma_start(out=vstg[:], in_=vflag[slot, :, :])
            G["vones_dma"] = s.done(inst)
        emit("sync", fvone)

        def fvone_v(s):
            s.wait(G["vones_dma"], G["vread_prev"])
            inst = s.eng.tensor_copy(out=_sv(vbar, 64, [[NH * 65, C], [65, NH], [1, 1]]),
                                     in_=_sv(vstg, 0, [[NH, C], [1, NH], [0, 1]]))
            G["vones"] = s.done(inst)
            G["vstg_done"] = G["vones"]
        emit("vector", fvone_v)

        if STAGE == 0:
            if own:
                j0 = slot - pre_ch

                def fout0(s, slot=slot, j0=j0):
                    s.wait(G["xload"][slot], G["out_last"])
                    G["out_last"] = s.done(s.eng.dma_start(out=out[j0 * C:(j0 + 1) * C, :], in_=xb[:]))
                    G["xfree"][slot] = G["out_last"]
                emit("sync", fout0)
            else:
                def fpre0(s, slot=slot):
                    G["xfree"][slot] = G["xload"][slot]
                program.append(("vector", fpre0))
            return
        if slot == 0:
            tap("px", xb, wait=lambda slot=slot: G["xload"][slot])
        ltoks = ln_chunk(xb, ln1, None, lambda slot=slot: G["xload"][slot])
        if slot == 0:
            tap("pln1", ln1, wait=lambda: ltoks["ln"])
        if not own:
            def fxf(s, slot=slot, ltoks=ltoks):
                G["xfree"][slot] = ltoks["ln"]
            program.append(("vector", fxf))
        if own and slot == pre_ch:
            tap("ln1", ln1)
        ttoks = transpose_blk(ln1, 0, ln1T, 8, lambda: ltoks["ln"])

        def fln1tr(s):
            G["ln1_tr_prev"] = ttoks["tr"]
        program.append(("vector", fln1tr))

        def fproj(s):
            for t in G.get("bar") or ():
                s.wait(t)
            s.wait(ttoks["ev"], G["attnw"], G["const"], G["pg_free"], G["state_add"])
            s.eng.drain()
            inst = None
            for k in range(8):
                lhs = ln1T[:, k * C:(k + 1) * C]
                if own:
                    s.eng.matmul(out=p_proj[:, 0:256], lhsT=lhs, rhs=wq_sb[:, k * 256:(k + 1) * 256],
                                 start=(k == 0), stop=(k == 7))
                s.eng.matmul(out=p_da[:, 0:256], lhsT=lhs, rhs=wk_sb[:, k * 256:(k + 1) * 256],
                             start=(k == 0), stop=(k == 7))
                s.eng.matmul(out=p_g[:], lhsT=lhs, rhs=wv_sb[:, k * 1024:k * 1024 + 512],
                             start=(k == 0), stop=(k == 7))
                inst = s.eng.matmul(out=p_u[:], lhsT=lhs, rhs=wv_sb[:, k * 1024 + 512:(k + 1) * 1024],
                                    start=(k == 0), stop=(k == 7))
            toks["proj"] = s.done(inst)
        emit("tensor", fproj)

        if slot == 0:
            tap("pagg", agg, wait=lambda: ltoks["ln"])

        def fevac(s):
            s.wait(toks["proj"], G["vones"], G.get("ktr_last"), G.get("qtr_last"))
            if own:
                s.eng.tensor_scalar(out=_sv(qtok, 0, [[512, C], [32, NH], [1, FD]]),
                                    in0=_sv(p_proj, 0, [[512, C], [FD, NH], [1, FD]]),
                                    scalar1=1.0, scalar2=None, op0=A.mult)
                s.eng.tensor_scalar(out=_sv(qtokH, 0, [[512, C], [32, NH], [1, FD]]),
                                    in0=_sv(p_proj, 0, [[512, C], [FD, NH], [1, FD]]),
                                    scalar1=0.5, scalar2=None, op0=A.mult)
            s.eng.tensor_scalar(out=_sv(ktok4, 0, [[512, C], [32, NH], [1, FD]]),
                                in0=_sv(p_da, 0, [[512, C], [FD, NH], [1, FD]]),
                                scalar1=0.25, scalar2=None, op0=A.mult)
            s.eng.tensor_copy(out=_sv(vbar, 0, [[NH * 65, C], [65, 8], [1, 64]]),
                              in_=_sv(p_g, 0, [[512, C], [64, 8], [1, 64]]))
            inst = s.eng.tensor_copy(out=_sv(vbar, 8 * 65, [[NH * 65, C], [65, 8], [1, 64]]),
                                     in_=_sv(p_u, 0, [[512, C], [64, 8], [1, 64]]))
            toks["evac"] = s.done(inst)
        emit("vector", fevac)

        if STAGE == 1:
            if own:
                j0 = slot - pre_ch

                def fout1(s, slot=slot, j0=j0):
                    s.wait(toks["evac"], G["out_last"], G["xload"][slot])
                    G["out_last"] = s.done(s.eng.dma_start(out=out[j0 * C:(j0 + 1) * C, :], in_=xb[:]))
                    G["xfree"][slot] = G["out_last"]
                emit("sync", fout1)
            else:
                def fpre1(s, slot=slot):
                    G["xfree"][slot] = toks["evac"]
                program.append(("vector", fpre1))

            def fpg1(s):
                G["pg_free"] = toks["evac"]
                G["vread_prev"] = toks["evac"]
            program.append(("vector", fpg1))
            return

        if slot == 0:
            tap("pvbar", vbar, wait=lambda: toks["evac"])
        def fkbar(s):
            s.gate(toks["evac"])
            inst = s.eng.tensor_copy(out=_sv(kbar, 0, [[272, C], [17, NH], [1, FD]]),
                                     in_=_sv(ktok4, 0, [[512, C], [32, NH], [1, FD]]))
            toks["kbar"] = s.done(inst)
        emit("vector", fkbar)

        for hh in range(2):
            def ftr(s, hh=hh):
                s.wait(toks["evac"], G["ptr_free"])
                s.eng.drain()
                last = None
                for t in range(8):
                    h = hh * 8 + t
                    last = s.eng.transpose(out=p_tr[0:32, t * C:(t + 1) * C],
                                           in_=ktok4[:, 32 * h:32 * h + 32], identity=idn[:])
                toks[f"ktr_{hh}"] = s.done(last)
            emit("tensor", ftr)

            def fte(s, hh=hh):
                s.wait(toks[f"ktr_{hh}"])
                inst = s.eng.activation(out=kTb[0:32, :], in_=p_tr[0:32, :],
                                        func=AF.Copy, bias=0.0, scale=1.0)
                toks[f"kte_{hh}"] = s.done(inst)
                G["ptr_free"] = toks[f"kte_{hh}"]
                G["ktr_last"] = toks[f"ktr_{hh}"]
            emit("scalar", fte)

            if own:
                def ftrq(s, hh=hh):
                    s.wait(toks["evac"], G["ptr_free"])
                    s.eng.drain()
                    last = None
                    for t in range(8):
                        h = hh * 8 + t
                        last = s.eng.transpose(out=p_tr[0:32, t * C:(t + 1) * C],
                                               in_=qtok[:, 32 * h:32 * h + 32], identity=idn[:])
                    toks[f"qtr_{hh}"] = s.done(last)
                emit("tensor", ftrq)

                def fteq(s, hh=hh):
                    s.wait(toks[f"qtr_{hh}"])
                    inst = s.eng.activation(out=qTb[0:32, :], in_=p_tr[0:32, :],
                                            func=AF.Copy, bias=0.0, scale=1.0)
                    toks[f"qte_{hh}"] = s.done(inst)
                    G["ptr_free"] = toks[f"qte_{hh}"]
                    G["qtr_last"] = toks[f"qtr_{hh}"]
                emit("scalar", fteq)

            for gh in range(2):
                g = hh * 2 + gh
                h0 = 4 * g

                def fkf2(s, g=g, h0=h0):
                    s.gate(toks["evac"])
                    s.wait(toks.get(f"delta_{g-1}"), G.get("delta_last") if g == 0 else None)
                    in0 = _sv(ktok4, 32 * h0, [[512, C], [32, 4], [1, FD], [0, FD]])
                    in1 = _sv(ktok4, 32 * h0, [[512, C], [32, 4], [0, FD], [1, FD]])
                    toks[f"kf2_{g}"] = s.done(s.eng.tensor_tensor(out=kf2[:], in0=in0, in1=in1, op=A.mult))
                emit("vector", fkf2)

                if own:
                    def fqf2(s, g=g, h0=h0):
                        s.gate(toks["evac"])
                        s.wait(toks.get(f"q2tr_{g-1}"), G.get("q2tr_last") if g == 0 else None)
                        in0 = _sv(qtokH, 32 * h0, [[512, C], [32, 4], [1, FD], [0, FD]])
                        in1 = _sv(qtok, 32 * h0, [[512, C], [32, 4], [0, FD], [1, FD]])
                        toks[f"qf2_{g}"] = s.done(s.eng.tensor_tensor(out=qf2[:], in0=in0, in1=in1, op=A.mult))
                    emit("vector", fqf2)

                    def fsraw(s, g=g, gh=gh, hh=hh):
                        s.wait(toks[f"kte_{hh}"], toks[f"qte_{hh}"], toks.get(f"yevac_{g-1}"), G["yevac_last"] if g == 0 else None)
                        s.eng.drain()
                        last = None
                        for si in range(4):
                            t = 4 * gh + si
                            last = s.eng.matmul(out=p_sr[:, si * C:(si + 1) * C],
                                                lhsT=kTb[0:FD, t * C:(t + 1) * C],
                                                rhs=qTb[0:FD, t * C:(t + 1) * C],
                                                start=True, stop=True)
                        toks[f"sraw_{g}"] = s.done(last)
                    emit("tensor", fsraw)

                    def fpoly(s, g=g):
                        s.wait(toks[f"sraw_{g}"], toks.get(f"abar_{g-1}"))
                        inst = s.eng.activation(out=upoly[:], in_=p_sr[:], func=AF.Square,
                                                bias=1.0, scale=1.0)
                        toks[f"poly_{g}"] = s.done(inst)
                        G["poly_last"] = toks[f"poly_{g}"]
                    emit("scalar", fpoly)

                    def fabar(s, g=g):
                        s.wait(toks[f"poly_{g}"])
                        inst = s.eng.scalar_tensor_tensor(out=abar[:], in0=upoly[:], scalar=1.0,
                                                          in1=_sv(msk, 0, [[C, C], [0, 4], [1, C]]),
                                                          op0=A.add, op1=A.mult)
                        toks[f"abar_{g}"] = s.done(inst)
                    emit("vector", fabar)

                    q2tt = transpose_blk(qf2, 0, qf2T, 8, lambda g=g: toks[f"qf2_{g}"])

                    def fq2tr(s, g=g, q2tt=q2tt):
                        toks[f"q2tr_{g}"] = q2tt["tr"]
                        G["q2tr_last"] = q2tt["tr"]
                    program.append(("vector", fq2tr))

                    def fy(s, g=g, gh=gh, q2tt=q2tt):
                        s.wait(toks[f"abar_{g}"], q2tt["ev"], G["state_add"], G["yevac_last"])
                        s.eng.drain()
                        last = None
                        for si in range(4):
                            h = 4 * g + si
                            yc = si * C
                            s.eng.matmul(out=p_y[:, yc:yc + 65], lhsT=abar[:, si * C:(si + 1) * C],
                                         rhs=vbar[:, h * 65:(h + 1) * 65], start=True, stop=False)
                            s.eng.matmul(out=p_y[:, yc:yc + 65], lhsT=qf2T[:, si * 256:si * 256 + C],
                                         rhs=s_allb[:, 1040 + h * 65:1040 + (h + 1) * 65],
                                         start=False, stop=False)
                            last = s.eng.matmul(out=p_y[:, yc:yc + 65],
                                                lhsT=qf2T[:, si * 256 + C:(si + 1) * 256],
                                                rhs=s_allb[:, 2080 + h * 65:2080 + (h + 1) * 65],
                                                start=False, stop=True)
                        toks[f"y_{g}"] = s.done(last)
                    emit("tensor", fy)

                    def fy01(s, g=g, gh=gh):
                        s.wait(toks[f"poly_{g}"], toks[f"y_{g}"])
                        s.eng.drain()
                        last = None
                        for si in range(4):
                            t = 4 * gh + si
                            h = 4 * g + si
                            last = s.eng.matmul(out=p_sr[:, si * C:si * C + 65],
                                                lhsT=qTb[0:17, t * C:(t + 1) * C],
                                                rhs=s_allb[0:17, h * 65:(h + 1) * 65],
                                                start=True, stop=True)
                        toks[f"y01_{g}"] = s.done(last)
                        G["y_read"] = toks[f"y01_{g}"]
                    emit("tensor", fy01)

                    def fz(s, g=g):
                        s.wait(toks[f"y01_{g}"], G["out_last"])
                        if g == 0:
                            s.wait(G.get("ytok_tr_prev"))
                        tok = s.done(s.eng.tensor_copy(out=dnm[:, 0:4],
                                                       in_=_sv(p_sr, 64, [[512, C], [C, 4]])))
                        s.eng.wait_ge(CV.sem, tok[1])
                        tok = s.done(s.eng.scalar_tensor_tensor(out=dnm[:, 4:8],
                                                                in0=_sv(p_y, 64, [[512, C], [C, 4]]),
                                                                scalar=1e-12, in1=dnm[:, 0:4],
                                                                op0=A.add, op1=A.add))
                        s.eng.wait_ge(CV.sem, tok[1])
                        tok = s.done(s.eng.reciprocal(out=zrec[:, 0:4], in_=dnm[:, 4:8]))
                        s.eng.wait_ge(CV.sem, tok[1])
                        inst = None
                        for si in range(4):
                            h = 4 * g + si
                            s.eng.tensor_scalar(out=x3[:, 0:64], in0=p_sr[:, si * C:si * C + 64],
                                                scalar1=zrec[:, si:si + 1], scalar2=None, op0=A.mult)
                            inst = s.eng.scalar_tensor_tensor(out=ytok[:, h * 64:(h + 1) * 64],
                                                              in0=p_y[:, si * C:si * C + 64],
                                                              scalar=zrec[:, si:si + 1],
                                                              in1=x3[:, 0:64], op0=A.mult, op1=A.add)
                        toks[f"yevac_{g}"] = s.done(inst)
                        G["yevac_last"] = toks[f"yevac_{g}"]
                    emit("vector", fz)

                def fdelta(s, g=g):
                    if STAGE == 2 and ASUB < 3:
                        return
                    s.wait(toks[f"kf2_{g}"], toks["kbar"], G["y_read"], toks.get(f"dadd_{g-1}"))
                    s.eng.drain()
                    for si in range(4):
                        h = 4 * g + si
                        s.eng.matmul(out=p_db[0:17, si * 65:si * 65 + 65],
                                     lhsT=kbar[:, 17 * h:17 * h + 17],
                                     rhs=vbar[:, h * 65:(h + 1) * 65], start=True, stop=True)
                    s.eng.drain()
                    last = None
                    for si in range(4):
                        h = 4 * g + si
                        s.eng.matmul(out=p_da[:, si * 65:(si + 1) * 65],
                                     lhsT=kf2[:, si * 256:si * 256 + C],
                                     rhs=vbar[:, h * 65:(h + 1) * 65], start=True, stop=True)
                        dst = p_da[:, 260 + si * 65:260 + (si + 1) * 65] if si < 3 else p_db[:, 260:325]
                        last = s.eng.matmul(out=dst,
                                            lhsT=kf2[:, si * 256 + C:(si + 1) * 256],
                                            rhs=vbar[:, h * 65:(h + 1) * 65], start=True, stop=True)
                    toks[f"delta_{g}"] = s.done(last)
                    G["delta_last"] = toks[f"delta_{g}"]
                emit("tensor", fdelta)

                def fadd(s, g=g):
                    if STAGE == 2 and ASUB < 3:
                        if g == NGRP - 1:
                            G["state_add"] = toks.get("evac")
                        return
                    s.wait(toks[f"delta_{g}"])
                    c0 = 4 * g * 65
                    s.eng.tensor_tensor(out=s_all[0:17, c0:c0 + 260], in0=s_all[0:17, c0:c0 + 260],
                                        in1=p_db[0:17, 0:260], op=A.add)
                    s.eng.tensor_tensor(out=s_all[:, 1040 + c0:1040 + c0 + 260],
                                        in0=s_all[:, 1040 + c0:1040 + c0 + 260],
                                        in1=p_da[:, 0:260], op=A.add)
                    s.eng.tensor_tensor(out=s_all[:, 2080 + c0:2080 + c0 + 195],
                                        in0=s_all[:, 2080 + c0:2080 + c0 + 195],
                                        in1=p_da[:, 260:455], op=A.add)
                    inst = s.eng.tensor_tensor(out=s_all[:, 2080 + c0 + 195:2080 + c0 + 260],
                                               in0=s_all[:, 2080 + c0 + 195:2080 + c0 + 260],
                                               in1=p_db[:, 260:325], op=A.add)
                    toks[f"dadd_{g}"] = s.done(inst)
                    if g == NGRP - 1:
                        inst = s.eng.tensor_copy(out=s_allb[:], in_=s_all[:])
                        toks[f"dadd_{g}"] = s.done(inst)
                    G["state_add"] = toks[f"dadd_{g}"]
                emit("vector", fadd)

        if not own:
            def fend(s):
                G["vread_prev"] = G["state_add"]
                G["pg_free"] = G["state_add"]
                if SLOTBAR:
                    G["bar"] = [(c, c.n) for c in (CT, CV, CS, CG, DS, DG)]
            program.append(("vector", fend))
            return

        ytt = transpose_blk(ytok, 0, yT, 8, lambda: toks["yevac_3"])

        def fytt(s):
            G["ytok_tr_prev"] = ytt["tr"]
        program.append(("vector", fytt))

        def fwo(s):
            s.wait(ytt["ev"])
            s.eng.drain()
            inst = None
            for k in range(8):
                s.eng.matmul(out=p_g[:], lhsT=yT[:, k * C:(k + 1) * C],
                             rhs=wo_sb[:, k * 1024:k * 1024 + 512], start=(k == 0), stop=(k == 7))
                inst = s.eng.matmul(out=p_u[:], lhsT=yT[:, k * C:(k + 1) * C],
                                    rhs=wo_sb[:, k * 1024 + 512:(k + 1) * 1024],
                                    start=(k == 0), stop=(k == 7))
            toks["wo"] = s.done(inst)
        emit("tensor", fwo)

        def fx2(s):
            s.wait(toks["wo"])
            s.eng.tensor_tensor(out=x2b[:, 0:512], in0=p_g[:], in1=xb[:, 0:512], op=A.add)
            inst = s.eng.tensor_tensor(out=x2b[:, 512:1024], in0=p_u[:], in1=xb[:, 512:1024], op=A.add)
            toks["x2"] = s.done(inst)
            G["xfree"][slot] = toks["x2"]
        emit("vector", fx2)
        if slot == pre_ch:
            tap("x2", x2b, wait=lambda: toks["x2"])
            tap("abar", abar, wait=lambda: toks["abar_3"])
            tap("oln1", ln1, wait=lambda: ltoks["ln"])
            tap("oln1T", ln1T, wait=lambda: ttoks["ev"])
            tap("wvsb", wv_sb, wait=lambda: G["attnw"])
            tap("ktok4", ktok4, wait=lambda: toks["evac"])
            tap("dnm", dnm, wait=lambda: toks["yevac_3"])
            tap("zrec", zrec, wait=lambda: toks["yevac_3"])
            tap("sallb", s_allb, wait=lambda: toks["yevac_3"])
            tap("qf2T", qf2T, wait=lambda: toks["yevac_3"])
            tap("ytok", ytok, wait=lambda: toks["yevac_3"])
            tap("vbar", vbar, wait=lambda: toks["evac"])

        if STAGE == 2 and ASUB == 1:
            j0 = slot - pre_ch

            def fouta(s, j0=j0, slot=slot):
                s.wait(toks["abar_3"], G["out_last"], G["xload"][slot])
                G["out_last"] = s.done(s.eng.dma_start(out=out[j0 * C:(j0 + 1) * C, :], in_=xb[:]))
                G["xfree"][slot] = G["out_last"]
            emit("sync", fouta)

            def fenda(s):
                G["vread_prev"] = toks["evac"]
                G["pg_free"] = toks["evac"]
            program.append(("vector", fenda))
            return
        if STAGE in (2, 3):
            j0 = slot - pre_ch

            def fout2(s, j0=j0):
                s.wait(toks["x2"], G["out_last"])
                G["out_last"] = s.done(s.eng.dma_start(out=out[j0 * C:(j0 + 1) * C, :], in_=x2b[:]))

            emit("sync", fout2)

            def fend2(s):
                G["vread_prev"] = G["state_add"]
                G["pg_free"] = toks["x2"]
            program.append(("vector", fend2))
            return
        l2 = ln_chunk(x2b, None, ln2, lambda: toks["x2"])

        def fq1(s):
            s.gate(l2["ln"])
            tok = s.done(s.eng.tensor_reduce(out=sc[:, 3:4], in_=ln2[:], axis=AX.X,
                                             op=A.max, apply_absolute_value=True))
            s.eng.wait_ge(CV.sem, tok[1])
            tok = s.done(s.eng.tensor_scalar(out=sc[:, 3:4], in0=sc[:, 3:4], scalar1=1e-5,
                                             scalar2=None, op0=A.max))
            s.eng.wait_ge(CV.sem, tok[1])
            tok = s.done(s.eng.reciprocal(out=sc[:, 4:5], in_=sc[:, 3:4]))
            s.eng.wait_ge(CV.sem, tok[1])
            s.eng.tensor_scalar(out=sc[:, 5:6], in0=sc[:, 4:5], scalar1=127.0, scalar2=None, op0=A.mult)
            tok = s.done(s.eng.tensor_scalar(out=sc[:, 6:7], in0=sc[:, 3:4], scalar1=1.0 / 127.0,
                                             scalar2=None, op0=A.mult))
            s.eng.wait_ge(CV.sem, tok[1])
            s.wait(G["out_last"])
            inst = None
            for half in range(2):
                cc = half * 512
                s.eng.tensor_scalar(out=x3[:, 0:512], in0=ln2[:, cc:cc + 512], scalar1=sc[:, 5:6],
                                    scalar2=128.0, op0=A.mult, op1=A.add)
                inst = s.eng.tensor_scalar(out=ytok[:, cc:cc + 512], in0=x3[:, 0:512], scalar1=R23,
                                           scalar2=-R23, op0=A.add, op1=A.add)
            toks["iq1"] = s.done(inst)
        emit("vector", fq1)

        itt = transpose_blk(ytok, 0, iq1T, 8, lambda: toks["iq1"], gkeys=("iq1T_free",), bias=-128.0)
        toks["iq1T"] = None

        def fiq1tok(s):
            toks["iq1T"] = itt["ev"]
        program.append(("vector", fiq1tok))

        def fsc2(s):
            s.wait(G["sw"].get("g"), G["sw"].get("u"), G["sw"].get("d"), G.get("silu_last"))
            s.gate(toks["iq1"])
            s.eng.tensor_tensor(out=sc[:, 7:8], in0=sc[:, 6:7], in1=sc[:, 12:13], op=A.mult)
            inst = s.eng.tensor_tensor(out=sc[:, 8:9], in0=sc[:, 6:7], in1=sc[:, 13:14], op=A.mult)
            tok = s.done(inst)
            s.eng.wait_ge(CV.sem, tok[1])
            toks["ffnsc"] = tok
        emit("vector", fsc2)

        for ot in range(8):
            def fgu(s, ot=ot):
                s.wait(toks.get("iq1T"), toks["ffnsc"], G["quant_done"], toks.get(f"gevac_{ot-1}"))
                if ot == 0:
                    s.eng.drain()
                inst = None
                for k in range(8):
                    s.eng.matmul(out=p_g[:], lhsT=iq1T[:, k * C:(k + 1) * C],
                                 rhs=iw_g[:, k * INNER + ot * 512:k * INNER + ot * 512 + 512],
                                 start=(k == 0), stop=(k == 7))
                    inst = s.eng.matmul(out=p_u[:], lhsT=iq1T[:, k * C:(k + 1) * C],
                                        rhs=iw_u[:, k * INNER + ot * 512:k * INNER + ot * 512 + 512],
                                        start=(k == 0), stop=(k == 7))
                toks[f"gu_{ot}"] = s.done(inst)
            emit("tensor", fgu)

            def fsilu(s, ot=ot):
                s.wait(toks[f"gu_{ot}"])
                inst = s.eng.activation(out=silu_s[:], in_=p_g[:], func=AF.Silu,
                                        bias=0.0, scale=sc[:, 7:8])
                toks[f"silu_{ot}"] = s.done(inst)
                if ot == 7:
                    G["silu_last"] = toks[f"silu_{ot}"]
            emit("scalar", fsilu)

            def fgpp(s, ot=ot):
                s.wait(toks[f"silu_{ot}"])
                inst = s.eng.scalar_tensor_tensor(out=gpp[:, ot * 512:(ot + 1) * 512],
                                                  in0=p_u[:], scalar=1.0 / 64.0,
                                                  in1=silu_s[:], op0=A.mult, op1=A.mult)
                toks[f"gevac_{ot}"] = s.done(inst)
            emit("vector", fgpp)

        def fq2(s):
            s.gate(toks["gevac_7"])
            tok = s.done(s.eng.tensor_reduce(out=sc[:, 9:10], in_=gpp[:], axis=AX.X,
                                             op=A.max, apply_absolute_value=True))
            s.eng.wait_ge(CV.sem, tok[1])
            tok = s.done(s.eng.tensor_tensor(out=sc[:, 10:11], in0=sc[:, 9:10],
                                             in1=sc[:, 8:9], op=A.mult))
            s.eng.wait_ge(CV.sem, tok[1])
            tok = s.done(s.eng.tensor_scalar(out=sc[:, 10:11], in0=sc[:, 10:11], scalar1=64.0,
                                             scalar2=1e-5, op0=A.mult, op1=A.max))
            s.eng.wait_ge(CV.sem, tok[1])
            tok = s.done(s.eng.reciprocal(out=sc[:, 11:12], in_=sc[:, 10:11]))
            s.eng.wait_ge(CV.sem, tok[1])
            tok = s.done(s.eng.tensor_tensor(out=sc[:, 0:1], in0=sc[:, 11:12], in1=sc[:, 8:9], op=A.mult))
            s.eng.wait_ge(CV.sem, tok[1])
            tok = s.done(s.eng.tensor_scalar(out=sc[:, 0:1], in0=sc[:, 0:1], scalar1=127.0 * 64.0,
                                             scalar2=None, op0=A.mult))
            s.eng.wait_ge(CV.sem, tok[1])
            inst = None
            for blkk in range(8):
                cc = blkk * 512
                s.eng.tensor_scalar(out=x3[:, 0:512], in0=gpp[:, cc:cc + 512], scalar1=sc[:, 0:1],
                                    scalar2=128.0, op0=A.mult, op1=A.add)
                inst = s.eng.tensor_scalar(out=gpp[:, cc:cc + 512], in0=x3[:, 0:512], scalar1=R23,
                                           scalar2=-R23, op0=A.add, op1=A.add)
            toks["iqg"] = s.done(inst)
            tok = s.done(s.eng.tensor_tensor(out=sc[:, 15:16], in0=sc[:, 10:11], in1=sc[:, 14:15], op=A.mult))
            s.eng.wait_ge(CV.sem, tok[1])
            tok = s.done(s.eng.tensor_scalar(out=sc[:, 15:16], in0=sc[:, 15:16],
                                             scalar1=1.0 / 127.0, scalar2=None, op0=A.mult))
            s.eng.wait_ge(CV.sem, tok[1])
            toks["dnsc"] = tok
        emit("vector", fq2)

        for rnd in range(4):
            def fdl(s, rnd=rnd):
                s.wait(G.get("dq_store"), G.get(f"dnbuf_free_{rnd % 2}"))
                inst = s.eng.dma_start(
                    out=iw_dbuf[rnd % 2][:],
                    in_=_sv(iw_dh, rnd * 8 * C * D, [[D, C], [C * D, 8], [1, D]]))
                toks[f"dload_{rnd}"] = s.done(inst)
            emit("sync", fdl)
            dtt = transpose_blk(gpp, rnd * 1024, iqgT[rnd % 2], 8, lambda: toks["iqg"], bias=-128.0)

            def fdn(s, rnd=rnd, dtt=dtt):
                s.wait(dtt["ev"], toks[f"dload_{rnd}"])
                s.eng.drain()
                inst = None
                for i in range(8):
                    k = rnd * 8 + i
                    s.eng.matmul(out=p_g[:], lhsT=iqgT[rnd % 2][:, i * C:(i + 1) * C],
                                 rhs=iw_dbuf[rnd % 2][:, i * D:i * D + 512],
                                 start=(k == 0), stop=(k == 31))
                    inst = s.eng.matmul(out=p_u[:], lhsT=iqgT[rnd % 2][:, i * C:(i + 1) * C],
                                        rhs=iw_dbuf[rnd % 2][:, i * D + 512:(i + 1) * D],
                                        start=(k == 0), stop=(k == 31))
                toks[f"dnmm_{rnd}"] = s.done(inst)

            def fdnfree(s, rnd=rnd):
                G[f"dnbuf_free_{rnd % 2}"] = toks[f"dnmm_{rnd}"]
            emit("tensor", fdn)
            program.append(("vector", fdnfree))

        def fx3(s):
            s.wait(toks["dnmm_3"], G["out_last"])
            s.gate(toks["dnsc"], toks["x2"])
            s.eng.scalar_tensor_tensor(out=x3[:, 0:512], in0=p_g[:], scalar=sc[:, 15:16],
                                       in1=x2b[:, 0:512], op0=A.mult, op1=A.add)
            inst = s.eng.scalar_tensor_tensor(out=x3[:, 512:1024], in0=p_u[:], scalar=sc[:, 15:16],
                                              in1=x2b[:, 512:1024], op0=A.mult, op1=A.add)
            toks["x3"] = s.done(inst)
        emit("vector", fx3)
        if slot == pre_ch:
            tap("x3", x3, wait=lambda: toks["x3"])
            tap("sc", sc, wait=lambda: toks["dnsc"])
            tap("swg", swg, wait=lambda: toks["dnsc"])
            tap("gpp", gpp, wait=lambda: toks["iqg"])
            tap("iq1T", iq1T, wait=lambda: toks["iq1T"])
            tap("ln2", ln2, wait=lambda: l2["ln"])
            tap("agg", agg, wait=lambda: l2["ln"])
            tap("bnb", bnb, wait=lambda: l2["ln"])
            tap("iwg", iw_g[:, 0:4096] if True else None, wait=lambda: G["quant_done"])

        j = slot - pre_ch

        def fout(s):
            s.wait(toks["x3"])
            G["out_last"] = s.done(s.eng.dma_start(out=out[j * C:(j + 1) * C, :], in_=x3[:]))
        emit("sync", fout)

        def fend(s):
            G["vread_prev"] = G["state_add"]
            G["pg_free"] = toks["x3"]
            G["iq1T_free"] = toks[f"dnmm_3"]
            G["dn_last"] = toks[f"dnmm_3"]
            if SLOTBAR:
                G["bar"] = [(c, c.n) for c in (CT, CV, CS, CG, DS, DG)]
        program.append(("vector", fend))

    emit_xload(0)
    emit_xload(1)
    p1_iter = list(p1_units)
    p2_iter = list(p2_units)
    fins_emitted = set()

    def pump_prep(n1, n2):
        for _ in range(n1):
            if not p1_iter:
                break
            u = p1_iter.pop(0)
            if u[0] == "fin":
                fins_emitted.add(u[1][0])
            emit_p1_unit(u)
        for _ in range(n2):
            if not p2_iter or p2_iter[0][0] not in fins_emitted:
                break
            emit_p2_unit(p2_iter.pop(0))

    if STAGE < 3:
        p1_iter, p2_iter = [], []
    if pre_ch < 24 or PREP_UPFRONT or "upfront" in str(DEBUG_TAPS):
        pump_prep(len(p1_units), 0)
        pump_prep(0, len(p2_units))
    for slot in range(tot_ch):
        pump_prep(4, 3)
        emit_slot(slot)
        if slot + 2 < tot_ch:
            emit_xload(slot + 2)
    while p1_iter or p2_iter:
        pump_prep(4, 4)

    def epilogue(s):
        s.wait(G["out_last"], G["quant_done"])
    emit("sync", epilogue)

    streams = {k: Stream(k, v) for k, v in ctrs.items()}
    grouped = []
    for en, fn in program:
        if grouped and grouped[-1][0] == en:
            grouped[-1][1].append(fn)
        else:
            grouped.append((en, [fn]))
    with es:
        with nc.Block() as blk:
            for en, fns in grouped:
                def body(e, en=en, fns=fns):
                    s = streams[en]
                    s.eng = e
                    for fn in fns:
                        fn(s)
                getattr(blk, en)(body)
    return nc


def _host_inputs(L0_attn_x, Wq, Wk, Wv, Wo, w_gate, w_up, w_down, pre_ch, own_ch):
    x = np.ascontiguousarray(np.asarray(L0_attn_x, dtype=np.float32))
    consts = {
        "wq": np.ascontiguousarray(np.asarray(Wq, dtype=np.float32)),
        "wk": np.ascontiguousarray(np.asarray(Wk, dtype=np.float32)),
        "wv": np.ascontiguousarray(np.asarray(Wv, dtype=np.float32)),
        "wo": np.ascontiguousarray(np.asarray(Wo, dtype=np.float32)),
        "wgT": np.ascontiguousarray(np.asarray(w_gate, dtype=np.float32).T),
        "wuT": np.ascontiguousarray(np.asarray(w_up, dtype=np.float32).T),
        "wdT": np.ascontiguousarray(np.asarray(w_down, dtype=np.float32).T),
        "ident": np.eye(C, dtype=ml_dtypes.bfloat16),
        "ones_pad": np.ones((C, 16), dtype=ml_dtypes.bfloat16),
        "zeros_pad": np.zeros((C, 32), dtype=ml_dtypes.bfloat16),
        "zstate": np.zeros((C, 1040), dtype=np.float32),
    }
    m = np.zeros((C, C), dtype=ml_dtypes.bfloat16)
    for jj in range(C):
        m[jj, jj:] = 0.5
    consts["maskM"] = m
    consts["onesf"] = np.ones((C, 1), dtype=np.float32)

    own_tok = own_ch * C
    in_maps = []
    for c in range(NCORE):
        b = c // 4
        i = c % 4
        start = i * own_tok
        npre_real = min(start // C, pre_ch)
        ndum = pre_ch - npre_real
        xp = np.zeros((max(pre_ch, 1) * C, D), dtype=np.float32)
        if npre_real:
            xp[ndum * C:pre_ch * C, :] = x[b, start - npre_real * C:start, :]
        vf = np.zeros((pre_ch + own_ch, C, NH), dtype=ml_dtypes.bfloat16)
        vf[ndum:, :, :] = 1.0
        im = dict(consts)
        im["x_own"] = np.ascontiguousarray(x[b, start:start + own_tok, :])
        im["x_pre"] = xp
        im["vflag"] = vf
        in_maps.append(im)
    return in_maps


def kernel(L0_attn_x, L0_attn_Wq, L0_attn_Wk, L0_attn_Wv, L0_attn_Wo,
           L0_ffn_w_gate, L0_ffn_w_up, L0_ffn_w_down):
    pre_ch, own_ch = (OVERRIDE_CH or (PRE_CH, OWN_CH))
    nc = build(pre_ch, own_ch, taps=DEBUG_TAPS)
    in_maps = _host_inputs(L0_attn_x, L0_attn_Wq, L0_attn_Wk, L0_attn_Wv, L0_attn_Wo,
                           L0_ffn_w_gate, L0_ffn_w_up, L0_ffn_w_down, pre_ch, own_ch)
    res = run_bass_kernel_spmd(nc, in_maps, core_ids=list(range(NCORE)))
    own_tok = own_ch * C
    outp = np.empty((B, 4 * own_tok, D), dtype=np.float32)
    for c in range(NCORE):
        b, i = c // 4, c % 4
        outp[b, i * own_tok:(i + 1) * own_tok, :] = res.results[c]["out"]
    if DEBUG_TAPS:
        kernel.last_results = res
    return outp
